# version 02 (final)
# baseline (speedup 1.0000x reference)
import numpy as np
import ml_dtypes
import concourse.bass as bass
import concourse.mybir as mybir
from concourse.bass import AP
from concourse.bass_utils import run_bass_kernel_spmd

F32 = mybir.dt.float32
BF16 = mybir.dt.bfloat16
I32 = mybir.dt.int32
AF = mybir.ActivationFunctionType
ALU = mybir.AluOpType
AX = mybir.AxisListType

T = 2048
D = 1024
import os as _os
REL_ON = _os.environ.get("KB_NOREL", "") == ""
KB_DBG = _os.environ.get("KB_DBG", "")
KB_ONLY = _os.environ.get("KB_ONLY", "")
NT = 16
NF = 8
H = 16
HD = 64
N_SUB = 8

BF16NP = ml_dtypes.bfloat16


class Buf:
    __slots__ = ("w", "rs", "name")

    def __init__(self, name=""):
        self.w = None
        self.rs = []
        self.name = name


class Prog:
    ENGINES = ("tensor", "vector", "scalar", "gpsimd", "sync")

    def __init__(self):
        self.ops = []
        self.barrier_idx = None

    def add(self, eng, fn, reads=(), writes=(), is_dma=False):
        idx = len(self.ops)
        dep = set()
        if self.barrier_idx is not None:
            dep.add(self.barrier_idx)
        for b in reads:
            if b.w is not None:
                dep.add(b.w)
        for b in writes:
            for r in b.rs:
                dep.add(r)
            if b.w is not None:
                dep.add(b.w)
        dep.discard(idx)
        self.ops.append(dict(eng=eng, fn=fn, deps=dep, is_dma=is_dma,
                             inc=False, sem=None, val=None))
        for b in reads:
            b.rs.append(idx)
        for b in writes:
            b.w = idx
            b.rs = []
        return idx

    NSLOT = 8

    def emit_plan(self):
        ops = self.ops
        SEQ = ("vector", "scalar")
        for op in ops:
            if op["is_dma"]:
                op["inc"] = True
            for d in op["deps"]:
                dop = ops[d]
                if (dop["is_dma"] or dop["eng"] != op["eng"]
                        or (op["eng"] in SEQ and not op["is_dma"])):
                    dop["inc"] = True
        counters = {}
        qidx = {}
        for op in ops:
            if not op["inc"]:
                continue
            if op["is_dma"]:
                i = qidx.get(op["eng"], 0)
                qidx[op["eng"]] = i + 1
                key = ("d", op["eng"], i % self.NSLOT)
                counters[key] = counters.get(key, 0) + 16
                op["sem"] = key
                op["val"] = counters[key]
            else:
                key = ("c", op["eng"])
                counters[key] = counters.get(key, 0) + 1
                op["sem"] = key
                op["val"] = counters[key]
        self.counters = counters

    def emit_engine(self, engine_name, eng, sems):
        ops = self.ops
        last_waited = {}
        for i, op in enumerate(ops):
            if op["eng"] != engine_name:
                continue
            waits = {}
            for d in op["deps"]:
                dop = ops[d]
                same = (not dop["is_dma"]) and dop["eng"] == engine_name
                if same and not (engine_name in ("vector", "scalar")
                                 and not op["is_dma"]):
                    continue
                if same and dop["sem"] is None:
                    continue
                key = dop["sem"]
                v = dop["val"]
                if waits.get(key, 0) < v:
                    waits[key] = v
            if op["is_dma"] and op["sem"] is not None:
                key = op["sem"]
                lim = op["val"] - 16
                if lim > 0 and last_waited.get(key, 0) < lim:
                    waits[key] = max(waits.get(key, 0), lim)
            for key in sorted(waits, key=lambda k: (k[0], k[1])):
                v = waits[key]
                if last_waited.get(key, 0) < v:
                    eng.wait_ge(sems[key], v)
                    last_waited[key] = v
            inst = op["fn"](eng)
            if op["inc"]:
                step = 16 if op["is_dma"] else 1
                inst.then_inc(sems[op["sem"]], step)


def cdiv(a, b):
    return (a + b - 1) // b


def build_nc(n_sub=N_SUB):
    nc = bass.Bass("TRN2", target_bir_lowering=False, debug=False)
    p = Prog()

    ins = {
        "x": [T, D],
        "L0_wqT_hm": [H, D, HD], "L0_wkT_hm": [H, D, HD], "L0_wvT_hm": [H, D, HD],
        "L0_woT_r": [HD, H * D],
        "L0_pekT": [HD, 1025], "L0_pevr": [1024, HD],
        "L0_pev0c": [HD, 1], "L0_pev0r": [1, HD],
        "L0_w": [D, D], "L0_a1": [D, D], "L0_a2": [D, D], "L0_bias": [1, D],
        "L1_wqT_hm": [H, D, HD], "L1_wkT_hm": [H, D, HD], "L1_wvT_hm": [H, D, HD],
        "L1_woT_r": [HD, H * D],
        "L1_bq_r": [HD, H], "L1_bk_r": [HD, H], "L1_bv_r": [HD, H], "L1_bo": [1, D],
        "L1_wqsT": [HD, 3], "L1_wksT": [HD, 3], "L1_wvsT": [HD, 3],
        "L1_bqs_c": [3, 1], "L1_bks_r": [1, 3], "L1_bvs_r": [1, 3],
        "L1_temp": [1, 1],
        "L1_rw": [D, 8], "L1_w1": [8, D, 2816], "L1_w2": [8, 1408, D],
        "L2_wq_nat": [D, D], "L2_wk_nat": [D, D], "L2_wvT": [D, D], "L2_woT": [D, D],
        "L2_wgT": [D, 4096], "L2_wuT": [D, 4096], "L2_wdT": [4096, D],
        "L3_wKT": [D, D], "L3_w1T": [D, HD], "L3_w2T": [HD, D],
        "L3_wrT": [D, D], "L3_wkT3": [D, D], "L3_wvT3": [D, D],
        "L3_mur": [128, NF], "L3_muk": [128, NF],
        "c_tril": [128, 128], "c_ninf": [128, 128],
    }
    dr = {}
    for name, shape in ins.items():
        dr[name] = nc.dram_tensor(name, shape, F32, kind="ExternalInput").ap()
    dr["c_ones"] = nc.dram_tensor("c_ones", [128, 128], BF16, kind="ExternalInput").ap()
    dr["c_trilu"] = nc.dram_tensor("c_trilu", [128, 128], mybir.dt.uint8, kind="ExternalInput").ap()
    dr["c_trilb"] = nc.dram_tensor("c_trilb", [128, 128], BF16, kind="ExternalInput").ap()
    out = nc.dram_tensor("out", [T, D], F32, kind="ExternalOutput").ap()

    QPSZ = 128 * 2049 + 128
    qp_d = nc.dram_tensor("qp_d", [2 * QPSZ], BF16, kind="Internal").ap()
    pb_d = nc.dram_tensor("pb_d", [2 * 128 * 3072], BF16, kind="Internal").ap()
    att_d = nc.dram_tensor("att_d", [NT * H * HD * 128], BF16, kind="Internal").ap()
    cr_d = nc.dram_tensor("cr_d", [2 * 3 * T], BF16, kind="Internal").ap()
    wq_d = nc.dram_tensor("wq_d", [3 * 4096 * D], BF16, kind="Internal").ap()
    bc_d = nc.dram_tensor("bc_d", [16], F32, kind="Internal").ap()

    AB = 16384 * 3 + 4352
    arena = nc.alloc_sbuf_tensor("arena", [128, AB], BF16).ap()
    aren32 = nc.alloc_sbuf_tensor("aren32", [128, 4096], F32).ap()
    areni = nc.alloc_sbuf_tensor("areni", [128, 1024], I32).ap()
    xres = nc.alloc_sbuf_tensor("xres", [128, NT * D], F32).ap()
    xbfs = [nc.alloc_sbuf_tensor("xbfA", [128, D], BF16).ap(),
            nc.alloc_sbuf_tensor("xbfB", [128, D], BF16).ap()]
    ts1024 = nc.alloc_sbuf_tensor("ts1024", [128, D], BF16).ap()
    ptb_t = nc.alloc_sbuf_tensor("ptb_t", [128, D], BF16).ap()
    prb_t = nc.alloc_sbuf_tensor("prb_t", [128, 2048], BF16).ap()
    prt_t = nc.alloc_sbuf_tensor("prt_t", [128, 2048], BF16).ap()
    ones_bf = nc.alloc_sbuf_tensor("ones_bf", [128, 128], BF16).ap()
    trilb = nc.alloc_sbuf_tensor("trilb", [128, 128], BF16).ap()

    ninf_f = nc.alloc_sbuf_tensor("ninf_f", [128, 128], F32).ap()
    tril32 = nc.alloc_sbuf_tensor("tril32", [128, 128], F32).ap()
    small = nc.alloc_sbuf_tensor("small", [128, 64], F32).ap()
    scw = nc.alloc_sbuf_tensor("scw", [128, 8], F32).ap()
    dml = nc.alloc_sbuf_tensor("dml", [128, 192], F32).ap()
    localp = nc.alloc_sbuf_tensor("localp", [128, 8 * NT], F32).ap()
    sxcol = nc.alloc_sbuf_tensor("sxcol", [128, 48], F32).ap()

    XT = 0
    Z1 = 16384
    Z2 = 32768
    Z3 = 49152

    S_STATS, S_MV = 0, 12
    S_NEGMAX, S_SUM, S_RECIP, S_QPC, S_TCOL = 16, 17, 18, 19, 20
    S_MW = 24
    S_MXA, S_SX, S_EB, S_M1 = 28, 29, 30, 31
    S_ACC, S_TOT, S_ONE, S_M2 = 32, 33, 34, 35
    S_RS = 0
    S_PART = 1
    S_BQ, S_BK, S_BV = 64, 80, 96
    S_BQS, S_PEV0, S_ROW = 112, 113, 114
    S_MU, S_MUK = 128, 136
    S_MU1, S_MUK1 = 144, 152

    bufs = {}

    def B(name):
        if name not in bufs:
            bufs[name] = Buf(name)
        return bufs[name]

    def dma(q, dst, src, reads, writes, slow=False):
        p.add(q, lambda e, dd=dst, ss=src, sl=slow:
              e.dma_start(out=dd, in_=ss, allow_slow_non_contiguous=sl),
              reads=reads, writes=writes, is_dma=True)

    def xpose(dst, src, reads, writes):
        p.add("sync", lambda e, dd=dst, ss=src: e.dma_start_transpose(out=dd, in_=ss),
              reads=reads, writes=writes, is_dma=True)

    def mm(out_ap, lhsT, rhs, start, stop, reads, writes):
        p.add("tensor",
              lambda e, o=out_ap, l=lhsT, r=rhs, st=start, sp=stop:
              e.matmul(o, l, r, start=st, stop=sp),
              reads=reads, writes=writes)

    def act(out_ap, in_ap, func, reads, writes, bias=0.0, scale=1.0, accum=None):
        if func == AF.Copy and not isinstance(bias, float):
            func = AF.Identity
        p.add("scalar",
              lambda e, o=out_ap, i=in_ap, f=func, b=bias, s=scale, a=accum:
              e.activation(out=o, in_=i, func=f, bias=b, scale=s, accum_out=a),
              reads=reads, writes=writes)

    def vop(name, reads, writes, **kw):
        p.add("vector", lambda e, n=name, k=dict(kw): getattr(e, n)(**k),
              reads=reads, writes=writes)

    def barrier():
        idx = p.add("vector", lambda e: e.nop(),
                    reads=[], writes=[b for b in bufs.values()])
        p.barrier_idx = idx

    def A(c0, w, r0=0, r1=128):
        return arena[r0:r1, c0:c0 + w]

    ps_s = nc.alloc_psum_tensor("ps_s", [128, 2048], F32).ap()
    ps_a = nc.alloc_psum_tensor("ps_a", [128, 1536], F32).ap()
    ps_o = nc.alloc_psum_tensor("ps_o", [128, 512], F32).ap()

    dma("sync",
        AP(xres.tensor, 0, [[NT * D, 128], [D, NT], [1, D]]),
        AP(dr["x"].tensor, 0, [[D, 128], [D * 128, NT], [1, D]]),
        reads=[], writes=[B("xres")])
    dma("sync", ones_bf[:, :], dr["c_ones"][:, :], reads=[], writes=[B("cones")])
    dma("sync", trilb[:, :], dr["c_trilb"][:, :], reads=[], writes=[B("ctrilb")])
    dma("sync", ninf_f[:, :], dr["c_ninf"][:, :], reads=[], writes=[B("cninf")])
    dma("sync", tril32[:, :], dr["c_tril"][:, :], reads=[], writes=[B("ctril32")])
    vop("memset", reads=[], writes=[B("onefc")],
        ap=small[:, S_ONE:S_ONE + 1], constant=1.0)
    vop("memset", reads=[], writes=[B("PRB")], ap=prb_t[:, :], constant=0.0)

    def cast_and_xpose(ti):
        s = ti % 2
        vop("tensor_copy", reads=[B("xres")], writes=[B(f"xbf{s}")],
            out=xbfs[s][:, :], in_=xres[:, ti * D:(ti + 1) * D])
        sv = xbfs[s].rearrange("p (j q) -> p j q", q=128)
        dv = AP(ts1024.tensor, 0, [[D, 128], [128, 8], [1, 128]])
        xpose(dv, sv, reads=[B(f"xbf{s}")], writes=[B("ts1024")])
        vop("tensor_copy", reads=[B("ts1024")], writes=[B("xT")],
            out=AP(arena.tensor, XT + ti * 128, [[AB, 128], [2048, 8], [1, 128]]),
            in_=AP(ts1024.tensor, 0, [[D, 128], [128, 8], [1, 128]]))

    for ti in range(NT):
        cast_and_xpose(ti)

    def xT_f(f, c0, c1):
        return arena[:, XT + f * 2048 + c0: XT + f * 2048 + c1]

    def ln_all(last):
        for ti in range(NT):
            xs = xres[:, ti * D:(ti + 1) * D]
            vop("bn_stats", reads=[B("xres")], writes=[B("lnst")],
                out=small[:, S_STATS:S_STATS + 6], in_=xs[:, 0:512])
            vop("bn_stats", reads=[B("xres")], writes=[B("lnst")],
                out=small[:, S_STATS + 6:S_STATS + 12], in_=xs[:, 512:1024])
            vop("bn_aggr", reads=[B("lnst")], writes=[B("lnmv")],
                out=small[:, S_MV:S_MV + 2], in_=small[:, S_STATS:S_STATS + 12])
            vop("tensor_scalar", reads=[B("lnmv")], writes=[B("lnr")],
                out=scw[:, S_RS:S_RS + 1], in0=small[:, S_MV + 1:S_MV + 2],
                scalar1=1e-5, scalar2=None, op0=ALU.add)
            act(scw[:, S_RS:S_RS + 1], scw[:, S_RS:S_RS + 1], AF.Ln,
                reads=[B("lnr")], writes=[B("lnr")])
            act(scw[:, S_RS:S_RS + 1], scw[:, S_RS:S_RS + 1], AF.Exp,
                reads=[B("lnr")], writes=[B("lnr")], scale=-0.5)
            vop("tensor_scalar", reads=[B("xres"), B("lnmv"), B("lnr")],
                writes=[B("xres")],
                out=xs, in0=xs, scalar1=small[:, S_MV:S_MV + 1],
                scalar2=scw[:, S_RS:S_RS + 1], op0=ALU.subtract, op1=ALU.mult)
            if not last:
                cast_and_xpose(ti)

    def attn_layer(L):
        pre = f"L{L}_"
        is_l1 = (L == 1)
        QA, KA, VH = Z2, Z2 + 2048, Z2 + 4096
        WSQ, WSK, WSV = Z2 + 5120, Z2 + 5632, Z2 + 6144
        PRB, PRT = Z2 + 6656, Z2 + 8704
        if L == 0:
            QP, QP0, PSB, PTB, PEK = (Z2 + 10752, Z2 + 11808, Z2 + 12832,
                                      Z2 + 13856, Z2 + 14880)
        else:
            SQZ, CKZ, K3Z = Z2 + 10752, Z2 + 12800, Z2 + 14848
        ATW, PEVR, AST, BO = Z3, Z3 + 2048, Z3 + 2560, Z3 + 2816
        BOB = BO
        SC = 0

        dma("gpsimd", AP(arena.tensor, Z1, [[AB, 64], [1, H * D]]),
            AP(dr[pre + "woT_r"].tensor, 0, [[H * D, 64], [1, H * D]]),
            reads=[], writes=[B("woT")])

        if L == 0:
            dma("gpsimd", AP(arena.tensor, PEK, [[AB, 64], [1, 1025]]),
                AP(dr["L0_pekT"].tensor, 0, [[1025, 64], [1, 1025]]),
                reads=[], writes=[B("PEK")])
            dma("gpsimd", AP(arena.tensor, PEVR, [[AB, 128], [HD, 8], [1, HD]]),
                AP(dr["L0_pevr"].tensor, 0, [[HD, 128], [128 * HD, 8], [1, HD]]),
                reads=[], writes=[B("PEVR")])
            dma("gpsimd", dml[0:64, S_PEV0:S_PEV0 + 1],
                AP(dr["L0_pev0c"].tensor, 0, [[1, 64], [1, 1]]),
                reads=[], writes=[B("pev0")])
            dma("gpsimd", arena[0:1, BO + 1024:BO + 1088],
                AP(dr["L0_pev0r"].tensor, 0, [[HD, 1], [1, HD]]),
                reads=[], writes=[B("pev0r")])
            mm(ps_o[:, 0:64], ones_bf[0:1, :], arena[0:1, BO + 1024:BO + 1088],
               True, True, reads=[B("pev0r"), B("cones")], writes=[B("ps_oz")])
            for bb in range(8):
                vop("tensor_tensor", reads=[B("PEVR"), B("ps_oz")], writes=[B("PEVR")],
                    out=A(PEVR + bb * 64, 64), in0=A(PEVR + bb * 64, 64),
                    in1=ps_o[:, 0:64], op=ALU.subtract)
            vop("memset", reads=[B("pev0r")], writes=[B("zbf")], ap=A(BO, 1024),
                constant=0.0)
            for bi in range(2):
                dma("scalar", AP(pb_d.tensor, bi * 128 * 3072, [[3072, 128], [1, 1024]]),
                    A(BO, 1024), reads=[B("zbf")], writes=[B(f"pbz{bi}")])

        if is_l1:
            dma("gpsimd", dml[0:1, S_ROW:S_ROW + 1],
                AP(dr["L1_temp"].tensor, 0, [[1, 1], [1, 1]]),
                reads=[], writes=[B("tempr")])
            vop("tensor_copy", reads=[B("tempr")], writes=[B("tempb")],
                out=arena[0:1, BO + 1088:BO + 1089], in_=dml[0:1, S_ROW:S_ROW + 1])
            mm(ps_o[:, 0:1], ones_bf[0:1, :], arena[0:1, BO + 1088:BO + 1089],
               True, True, reads=[B("tempb"), B("cones")], writes=[B("ps_oz")])
            vop("tensor_copy", reads=[B("ps_oz")], writes=[B("tcol")],
                out=small[:, S_TCOL:S_TCOL + 1], in_=ps_o[:, 0:1])
            for (nm, col) in (("L1_bq_r", S_BQ), ("L1_bk_r", S_BK), ("L1_bv_r", S_BV)):
                dma("gpsimd", dml[0:64, col:col + H],
                    AP(dr[nm].tensor, 0, [[H, 64], [1, H]]),
                    reads=[], writes=[B("battn")])
            dma("gpsimd", dml[0:3, S_BQS:S_BQS + 1],
                AP(dr["L1_bqs_c"].tensor, 0, [[1, 3], [1, 1]]),
                reads=[], writes=[B("battn")])
            dma("gpsimd", arena[0:1, K3Z + 256:K3Z + 259],
                AP(dr["L1_bks_r"].tensor, 0, [[3, 1], [1, 3]]),
                reads=[], writes=[B("bksr")])
            dma("gpsimd", arena[0:1, K3Z + 260:K3Z + 263],
                AP(dr["L1_bvs_r"].tensor, 0, [[3, 1], [1, 3]]),
                reads=[], writes=[B("bvsr")])
            for wn, off in (("L1_wqsT", 264), ("L1_wksT", 268), ("L1_wvsT", 272)):
                dma("gpsimd", arena[0:64, K3Z + off:K3Z + off + 3],
                    AP(dr[wn].tensor, 0, [[3, 64], [1, 3]]),
                    reads=[], writes=[B("wqs")])
            vop("tensor_copy", reads=[B("battn")], writes=[B("bvb")],
                out=arena[0:64, K3Z + 276:K3Z + 276 + H],
                in_=dml[0:64, S_BV:S_BV + H])
            dma("gpsimd", arena[0:1, BOB:BOB + D],
                AP(dr["L1_bo"].tensor, 0, [[D, 1], [1, D]]),
                reads=[], writes=[B("bob")])
            for ch in range(2):
                o = ps_o[0:1, 0:512]
                for hh in range(H):
                    mm(o, arena[0:64, K3Z + 276 + hh:K3Z + 277 + hh],
                       arena[0:64, Z1 + hh * D + ch * 512:Z1 + hh * D + ch * 512 + 512],
                       start=(hh == 0), stop=False,
                       reads=[B("bvb"), B("woT")], writes=[B("ps_oz")])
                mm(o, ones_bf[0:1, 0:1],
                   arena[0:1, BOB + ch * 512:BOB + (ch + 1) * 512],
                   start=False, stop=True,
                   reads=[B("bob"), B("cones")], writes=[B("ps_oz")])
                act(arena[0:1, BO + ch * 512:BO + (ch + 1) * 512],
                    ps_o[0:1, 0:512], AF.Copy,
                    reads=[B("ps_oz"), B("bob")], writes=[B("bop")])

        for h in range(H):
            if KB_ONLY and h != int(KB_ONLY.split(",")[0]):
                continue
            for (wn, zz, bb, ws, wsb, bcol) in (
                    (pre + "wqT_hm", QA, "QA", WSQ, "wsQ", S_BQ),
                    (pre + "wkT_hm", KA, "KA", WSK, "wsK", S_BK)):
                dma("gpsimd", AP(arena.tensor, ws, [[AB, 128], [HD, NF], [1, HD]]),
                    AP(dr[wn].tensor, h * D * HD, [[HD, 128], [128 * HD, NF], [1, HD]]),
                    reads=[], writes=[B(wsb)])
                for tc in range(4):
                    o = ps_o[0:64, 0:512]
                    for f in range(NF):
                        mm(o, A(ws + f * HD, HD), xT_f(f, tc * 512, tc * 512 + 512),
                           start=(f == 0), stop=(f == NF - 1),
                           reads=[B(wsb), B("xT")], writes=[B("ps_oz")])
                    kwb = {}
                    rd = [B("ps_oz")]
                    if is_l1:
                        kwb = dict(bias=dml[0:64, bcol + h:bcol + h + 1])
                        rd.append(B("battn"))
                    act(arena[0:64, zz + tc * 512:zz + (tc + 1) * 512],
                        ps_o[0:64, 0:512], AF.Copy, reads=rd, writes=[B(bb)], **kwb)
            dma("gpsimd", AP(arena.tensor, WSV, [[AB, 128], [HD, NF], [1, HD]]),
                AP(dr[pre + "wvT_hm"].tensor, h * D * HD,
                   [[HD, 128], [128 * HD, NF], [1, HD]]),
                reads=[], writes=[B("wsV")])
            for ti in range(NT):
                o = ps_o[:, 0:HD]
                for f in range(NF):
                    mm(o, xT_f(f, ti * 128, ti * 128 + 128), A(WSV + f * HD, HD),
                       start=(f == 0), stop=(f == NF - 1),
                       reads=[B("wsV"), B("xT")], writes=[B("ps_oz")])
                act(A(VH + ti * HD, HD), ps_o[:, 0:HD], AF.Copy,
                    reads=[B("ps_oz")], writes=[B("VH")])

            if is_l1:
                for tc in range(4):
                    o = ps_o[0:3, 0:512]
                    mm(o, arena[0:64, K3Z + 264:K3Z + 267],
                       A(QA + tc * 512, 512, 0, 64),
                       start=True, stop=True,
                       reads=[B("wqs"), B("QA")], writes=[B("ps_oz")])
                    act(arena[0:3, SQZ + tc * 512:SQZ + (tc + 1) * 512],
                        ps_o[0:3, 0:512], AF.Copy, bias=dml[0:3, S_BQS:S_BQS + 1],
                        reads=[B("ps_oz"), B("battn")], writes=[B("SQZ")])
                for (woff, brow, slot, bbuf) in ((268, 256, 0, "bksr"),
                                                 (272, 260, 1, "bvsr")):
                    for ti in range(NT):
                        o = ps_o[:, 0:3]
                        mm(o, A(KA + ti * 128, 128, 0, 64),
                           arena[0:64, K3Z + woff:K3Z + woff + 3],
                           start=True, stop=False,
                           reads=[B("wqs"), B("KA")], writes=[B("ps_oz")])
                        mm(o, ones_bf[0:1, 0:128],
                           arena[0:1, K3Z + brow:K3Z + brow + 3],
                           start=False, stop=True,
                           reads=[B(bbuf), B("cones")], writes=[B("ps_oz")])
                        vop("tensor_copy", reads=[B("ps_oz")], writes=[B(f"k3t{slot}")],
                            out=A(K3Z + slot * 48 + ti * 3, 3), in_=ps_o[:, 0:3])

                def col3(slot, c):
                    return AP(arena.tensor, K3Z + slot * 48 + c, [[AB, 128], [3, NT]])

                crs = K3Z + 96
                for c in range(3):
                    a1, a2 = (c + 1) % 3, (c + 2) % 3
                    vop("tensor_tensor", reads=[B("k3t0"), B("k3t1")], writes=[B("crt")],
                        out=AP(arena.tensor, crs + 48 + c, [[AB, 128], [3, NT]]),
                        in0=col3(0, a1), in1=col3(1, a2), op=ALU.mult)
                    vop("tensor_tensor", reads=[B("k3t0"), B("k3t1")], writes=[B("crt2")],
                        out=AP(arena.tensor, crs + 96 + c, [[AB, 128], [3, NT]]),
                        in0=col3(0, a2), in1=col3(1, a1), op=ALU.mult)
                    vop("tensor_tensor", reads=[B("crt"), B("crt2")], writes=[B("crx")],
                        out=AP(arena.tensor, crs + c, [[AB, 128], [3, NT]]),
                        in0=AP(arena.tensor, crs + 48 + c, [[AB, 128], [3, NT]]),
                        in1=AP(arena.tensor, crs + 96 + c, [[AB, 128], [3, NT]]),
                        op=ALU.subtract)
                hb = h % 2
                dma("scalar", AP(cr_d.tensor, hb * 3 * T, [[3, 128], [384, NT], [1, 3]]),
                    AP(arena.tensor, crs, [[AB, 128], [3, NT], [1, 3]]),
                    reads=[B("crx")], writes=[B(f"crd{hb}")])
                dma("scalar", AP(arena.tensor, CKZ, [[AB, 3], [1, T]]),
                    AP(cr_d.tensor, hb * 3 * T, [[1, 3], [3, T]]),
                    reads=[B(f"crd{hb}")], writes=[B("CKZ")], slow=True)

            for qi in range(NT):
                if KB_ONLY and qi != int(KB_ONLY.split(",")[1]):
                    continue
                q0 = qi * 128
                kw = (q0 + 128) if L == 0 else T
                nck = cdiv(kw, 512)
                for ck in range(nck):
                    c0, c1 = ck * 512, min(kw, (ck + 1) * 512)
                    mm(ps_s[:, c0:c1], A(QA + q0, 128, 0, 64), A(KA + c0, c1 - c0, 0, 64),
                       start=True, stop=(not is_l1),
                       reads=[B("QA"), B("KA")], writes=[B("ps_sz")])
                    if is_l1:
                        mm(ps_s[:, c0:c1], arena[0:3, SQZ + q0:SQZ + q0 + 128],
                           arena[0:3, CKZ + c0:CKZ + c1],
                           start=False, stop=True,
                           reads=[B("SQZ"), B("CKZ")], writes=[B("ps_sz")])
                if L == 0 and REL_ON:
                    jlo = max(0, 896 - q0)
                    for (c0, c1) in ((jlo, 512), (512, 1024), (1024, 1025)):
                        if c0 >= c1:
                            continue
                        mm(ps_a[:, c0:c1], A(QA + q0, 128, 0, 64),
                           arena[0:64, PEK + c0:PEK + c1],
                           start=True, stop=True,
                           reads=[B("QA"), B("PEK")], writes=[B("ps_az")])
                    vop("tensor_copy", reads=[B("ps_az")], writes=[B("qpc0")],
                        out=small[:, S_QPC:S_QPC + 1], in_=ps_a[:, jlo:jlo + 1])
                    vop("tensor_copy", reads=[B("ps_az")], writes=[B("QPb")],
                        out=A(QP + jlo, 1025 - jlo), in_=ps_a[:, jlo:1025])
                    vop("tensor_scalar", reads=[B("qpc0"), B("cones")], writes=[B("QP0")],
                        out=A(QP0, 1024),
                        in0=AP(ones_bf.tensor, 0, [[128, 128], [0, 1024]]),
                        scalar1=small[:, S_QPC:S_QPC + 1], scalar2=None, op0=ALU.mult)
                    qb = (h * NT + qi) % 2
                    qoff = qb * QPSZ
                    dma("sync", AP(qp_d.tensor, qoff, [[2049, 128], [1, 1024]]),
                        A(QP0, 1024), reads=[B("QP0")], writes=[B(f"qpd{qb}")])
                    dma("sync", AP(qp_d.tensor, qoff + 1024 + jlo,
                                   [[2049, 128], [1, 1025 - jlo]]),
                        A(QP + jlo, 1025 - jlo), reads=[B("QPb")], writes=[B(f"qpd{qb}")])
                    dma("scalar", A(PRT, kw),
                        AP(qp_d.tensor, qoff + 2048 - q0, [[2048, 128], [1, kw]]),
                        reads=[B(f"qpd{qb}")], writes=[B("PRT")])
                    vop("tensor_tensor", reads=[B("ps_sz"), B("PRT")], writes=[B("SC")],
                        out=aren32[:, SC:SC + kw], in0=ps_s[:, 0:kw],
                        in1=A(PRT, kw), op=ALU.add)
                    vop("tensor_tensor", reads=[B("SC"), B("ctril32")], writes=[B("SC")],
                        out=aren32[:, SC + q0:SC + q0 + 128],
                        in0=aren32[:, SC + q0:SC + q0 + 128], in1=tril32[:, :],
                        op=ALU.mult)
                    vop("tensor_tensor", reads=[B("SC"), B("cninf")], writes=[B("SC")],
                        out=aren32[:, SC + q0:SC + q0 + 128],
                        in0=aren32[:, SC + q0:SC + q0 + 128], in1=ninf_f[:, :],
                        op=ALU.add)
                    sc_src = aren32[:, SC:SC + kw]
                    sc_reads = [B("SC")]
                elif L == 0:
                    vop("tensor_copy", reads=[B("ps_sz")], writes=[B("SC")],
                        out=aren32[:, SC:SC + kw], in_=ps_s[:, 0:kw])
                    vop("tensor_tensor", reads=[B("SC"), B("ctril32")], writes=[B("SC")],
                        out=aren32[:, SC + q0:SC + q0 + 128],
                        in0=aren32[:, SC + q0:SC + q0 + 128], in1=tril32[:, :],
                        op=ALU.mult)
                    vop("tensor_tensor", reads=[B("SC"), B("cninf")], writes=[B("SC")],
                        out=aren32[:, SC + q0:SC + q0 + 128],
                        in0=aren32[:, SC + q0:SC + q0 + 128], in1=ninf_f[:, :],
                        op=ALU.add)
                    sc_src = aren32[:, SC:SC + kw]
                    sc_reads = [B("SC")]
                else:
                    sc_src = ps_s[:, 0:kw]
                    sc_reads = [B("ps_sz")]
                vop("reduce_max", reads=sc_reads, writes=[B("negmax")],
                    out=small[:, S_NEGMAX:S_NEGMAX + 1], in_=sc_src,
                    axis=AX.X, negate=True)
                if L == 0:
                    vop("tensor_scalar", reads=[B("negmax")], writes=[B("ebias")],
                        out=small[:, S_EB:S_EB + 1],
                        in0=small[:, S_NEGMAX:S_NEGMAX + 1],
                        scalar1=0.125, scalar2=None, op0=ALU.mult)
                    escale = 0.125
                    erd = []
                else:
                    vop("tensor_scalar", reads=[B("negmax"), B("tcol")], writes=[B("ebias")],
                        out=small[:, S_EB:S_EB + 1],
                        in0=small[:, S_NEGMAX:S_NEGMAX + 1],
                        scalar1=small[:, S_TCOL:S_TCOL + 1], scalar2=None, op0=ALU.mult)
                    escale = small[:, S_TCOL:S_TCOL + 1]
                    erd = [B("tcol")]
                for ck in range(nck):
                    c0, c1 = ck * 512, min(kw, (ck + 1) * 512)
                    src_ck = aren32[:, SC + c0:SC + c1] if L == 0 else ps_s[:, c0:c1]
                    act(prb_t[:, c0:c1], src_ck, AF.Exp,
                        reads=sc_reads + [B("ebias")] + erd,
                        writes=[B("PRB")], bias=small[:, S_EB:S_EB + 1],
                        scale=escale)
                vop("reduce_sum", reads=[B("PRB")], writes=[B("PRB")],
                    out=small[:, S_SUM:S_SUM + 1], in_=prb_t[:, 0:kw], axis=AX.X)
                vop("reciprocal", reads=[B("PRB")], writes=[B("recip")],
                    out=small[:, S_RECIP:S_RECIP + 1], in_=small[:, S_SUM:S_SUM + 1])
                vop("tensor_scalar", reads=[B("PRB"), B("recip")], writes=[B("PRB")],
                    out=prb_t[:, 0:kw], in0=prb_t[:, 0:kw],
                    scalar1=small[:, S_RECIP:S_RECIP + 1], scalar2=None, op0=ALU.mult)
                nkt = kw // 128
                if True and KB_DBG == f"{L},{h},{qi}":
                    dw = min(kw, 1024)
                    dma("sync", AP(out.tensor, 0, [[D, 128], [1, dw]]),
                        aren32[:, SC:SC + dw],
                        reads=sc_reads, writes=[B("dbg")])
                    dma("gpsimd", AP(out.tensor, 128 * D, [[D, 128], [1, kw]]),
                        prb_t[:, 0:kw], reads=[B("PRB")], writes=[B("dbg")])
                if KB_DBG == f"{L},{h},{qi}":
                    dma("sync", AP(out.tensor, 256 * D, [[D, 128], [1, 48]]),
                        small[:, 0:48], reads=[B("negmax"), B("ebias"), B("recip")],
                        writes=[B("dbg")])
                xpose(AP(prt_t.tensor, 0, [[2048, 128], [128, 16], [1, 128]]),
                      AP(prb_t.tensor, 0, [[2048, 128], [128, 16], [1, 128]]),
                      reads=[B("PRB")], writes=[B("PRTT")])
                if L == 0 and REL_ON:
                    pbuf = (h * NT + qi) % 2
                    poff = pbuf * 128 * 3072
                    lo = max(0, q0 - 1023)
                    dma("scalar", AP(pb_d.tensor, poff + 1024 + lo,
                                     [[3072, 128], [1, kw - lo]]),
                        prb_t[:, lo:kw], reads=[B("PRB")], writes=[B(f"pbd{pbuf}")])
                    dma("scalar", ts1024[:, :],
                        AP(pb_d.tensor, poff + q0 + 1, [[3073, 128], [1, 1024]]),
                        reads=[B(f"pbd{pbuf}"), B(f"pbz{pbuf}")], writes=[B("ts1024")])
                    xpose(AP(ptb_t.tensor, 0, [[D, 128], [128, 8], [1, 128]]),
                          AP(ts1024.tensor, 0, [[D, 128], [128, 8], [1, 128]]),
                          reads=[B("ts1024")], writes=[B("PTB")])
                o = ps_o[0:64, 0:128]
                for kt in range(nkt):
                    mm(o, A(VH + kt * HD, HD), prt_t[:, kt * 128:(kt + 1) * 128],
                       start=(kt == 0),
                       stop=((L != 0 or not REL_ON) and kt == nkt - 1),
                       reads=[B("VH"), B("PRTT")], writes=[B("ps_oz")])
                if L == 0 and REL_ON:
                    for bb in range(8):
                        mm(o, A(PEVR + bb * 64, 64), ptb_t[:, bb * 128:(bb + 1) * 128],
                           start=False, stop=(bb == 7),
                           reads=[B("PEVR"), B("PTB")], writes=[B("ps_oz")])
                sl = qi % 2
                relb = (L == 0 and REL_ON)
                kwb = dict(bias=dml[0:64, S_PEV0:S_PEV0 + 1]) if relb else {}
                act(A(AST + sl * 128, 128, 0, 64), ps_o[0:64, 0:128], AF.Copy,
                    reads=[B("ps_oz")] + ([B("pev0")] if relb else []),
                    writes=[B(f"ast{sl}")], **kwb)
                dma("sync", AP(att_d.tensor, (qi * H + h) * HD * 128,
                               [[128, 64], [1, 128]]),
                    A(AST + sl * 128, 128, 0, 64),
                    reads=[B(f"ast{sl}")], writes=[B("attd")])

        for qi in range(NT):
            if KB_ONLY:
                break
            dma("gpsimd", AP(arena.tensor, ATW, [[AB, 64], [128, H], [1, 128]]),
                AP(att_d.tensor, qi * H * HD * 128, [[128, 64], [HD * 128, H], [1, 128]]),
                reads=[B("attd")], writes=[B("ATW")])
            for ch in range(2):
                o = ps_s[:, ch * 512:(ch + 1) * 512]
                for hh in range(H):
                    mm(o, A(ATW + hh * 128, 128, 0, 64),
                       arena[0:64, Z1 + hh * D + ch * 512:Z1 + hh * D + ch * 512 + 512],
                       start=(hh == 0), stop=(hh == H - 1 and not is_l1),
                       reads=[B("ATW"), B("woT")], writes=[B("ps_sz")])
                if is_l1:
                    mm(o, ones_bf[0:1, 0:128],
                       arena[0:1, BO + ch * 512:BO + (ch + 1) * 512],
                       start=False, stop=True,
                       reads=[B("bop"), B("cones")], writes=[B("ps_sz")])
            vop("tensor_tensor", reads=[B("ps_sz"), B("xres")], writes=[B("xres")],
                out=xres[:, qi * D:(qi + 1) * D], in0=ps_s[:, 0:1024],
                in1=xres[:, qi * D:(qi + 1) * D], op=ALU.add)

    def ffn_l0():
        WST = Z3
        for (wn, src_zone, src_buf, dst_zone, bufn) in (
                ("L0_w", XT, "xT", Z1, "h1T"), ("L0_a1", Z1, "h1T", Z2, "h2T")):
            for dt_ in range(NF):
                sl = dt_ % 2
                ws = WST + sl * 1024
                dma("gpsimd", AP(arena.tensor, ws, [[AB, 128], [128, NF], [1, 128]]),
                    AP(dr[wn].tensor, dt_ * 128, [[D, 128], [128 * D, NF], [1, 128]]),
                    reads=[], writes=[B(f"wst{sl}")])
                for tc in range(4):
                    o = ps_o[:, 0:512]
                    for f in range(NF):
                        mm(o, A(ws + f * 128, 128),
                           arena[:, src_zone + f * 2048 + tc * 512:
                                 src_zone + f * 2048 + (tc + 1) * 512],
                           start=(f == 0), stop=(f == NF - 1),
                           reads=[B(f"wst{sl}"), B(src_buf)], writes=[B("ps_oz")])
                    act(arena[:, dst_zone + dt_ * 2048 + tc * 512:
                              dst_zone + dt_ * 2048 + (tc + 1) * 512],
                        ps_o[:, 0:512], AF.Copy, reads=[B("ps_oz")], writes=[B(bufn)])
        if _os.environ.get("KB_FFNSTOP", "") == "1":
            return
        for f in range(NF):
            dma("gpsimd", AP(arena.tensor, Z1 + f * 1024, [[AB, 128], [1, D]]),
                AP(dr["L0_a2"].tensor, f * 128 * D, [[D, 128], [1, D]]),
                reads=[], writes=[B("h1T")])
        dma("gpsimd", arena[0:1, Z3 + 2048:Z3 + 2048 + D],
            AP(dr["L0_bias"].tensor, 0, [[D, 1], [1, D]]),
            reads=[], writes=[B("biasr")])
        for ti in range(NT):
            for ch in range(2):
                o = ps_s[:, ch * 512:(ch + 1) * 512]
                for f in range(NF):
                    mm(o, arena[:, Z2 + f * 2048 + ti * 128:Z2 + f * 2048 + ti * 128 + 128],
                       arena[:, Z1 + f * 1024 + ch * 512:Z1 + f * 1024 + (ch + 1) * 512],
                       start=(f == 0), stop=False,
                       reads=[B("h2T"), B("h1T")], writes=[B("ps_sz")])
                mm(o, ones_bf[0:1, 0:128],
                   arena[0:1, Z3 + 2048 + ch * 512:Z3 + 2048 + (ch + 1) * 512],
                   start=False, stop=True,
                   reads=[B("biasr"), B("cones")], writes=[B("ps_sz")])
            act(aren32[:, 0:512], ps_s[:, 0:512], AF.Tanh,
                reads=[B("ps_sz")], writes=[B("SC")])
            act(aren32[:, 512:1024], ps_s[:, 512:1024], AF.Tanh,
                reads=[B("ps_sz")], writes=[B("SC")])
            vop("tensor_tensor", reads=[B("SC"), B("xres")], writes=[B("xres")],
                out=xres[:, ti * D:(ti + 1) * D], in0=aren32[:, 0:1024],
                in1=xres[:, ti * D:(ti + 1) * D], op=ALU.add)

    def ffn_l1():
        dma("gpsimd", AP(arena.tensor, Z3, [[AB, 128], [8, NF], [1, 8]]),
            AP(dr["L1_rw"].tensor, 0, [[8, 128], [128 * 8, NF], [1, 8]]),
            reads=[], writes=[B("rw")])
        for ti in range(NT):
            o = ps_o[:, 0:8]
            for f in range(NF):
                mm(o, xT_f(f, ti * 128, ti * 128 + 128), A(Z3 + f * 8, 8),
                   start=(f == 0), stop=(f == NF - 1),
                   reads=[B("rw"), B("xT")], writes=[B("ps_oz")])
            lp = localp[:, ti * 8:(ti + 1) * 8]
            vop("reduce_max", reads=[B("ps_oz")], writes=[B("negmax")],
                out=small[:, S_NEGMAX:S_NEGMAX + 1], in_=ps_o[:, 0:8],
                axis=AX.X, negate=True)
            act(aren32[:, 2048:2056], ps_o[:, 0:8], AF.Exp,
                reads=[B("ps_oz"), B("negmax")], writes=[B("rprob")],
                bias=small[:, S_NEGMAX:S_NEGMAX + 1])
            vop("reduce_sum", reads=[B("rprob")], writes=[B("rprob")],
                out=small[:, S_SUM:S_SUM + 1], in_=aren32[:, 2048:2056], axis=AX.X)
            vop("reciprocal", reads=[B("rprob")], writes=[B("recip")],
                out=small[:, S_RECIP:S_RECIP + 1], in_=small[:, S_SUM:S_SUM + 1])
            vop("tensor_scalar", reads=[B("rprob"), B("recip")], writes=[B("rprob")],
                out=aren32[:, 2048:2056], in0=aren32[:, 2048:2056],
                scalar1=small[:, S_RECIP:S_RECIP + 1], scalar2=None, op0=ALU.mult)
            vop("reduce_max", reads=[B("rprob")], writes=[B("m1")],
                out=small[:, S_M1:S_M1 + 1], in_=aren32[:, 2048:2056], axis=AX.X)
            vop("tensor_scalar", reads=[B("rprob"), B("m1")], writes=[B("eq1")],
                out=aren32[:, 2056:2064], in0=aren32[:, 2048:2056],
                scalar1=small[:, S_M1:S_M1 + 1], scalar2=None, op0=ALU.is_ge)
            vop("scalar_tensor_tensor", reads=[B("eq1"), B("rprob")], writes=[B("p2")],
                out=aren32[:, 2064:2072], in0=aren32[:, 2056:2064],
                scalar=-1e30, in1=aren32[:, 2048:2056], op0=ALU.mult, op1=ALU.add)
            vop("reduce_max", reads=[B("p2")], writes=[B("m2")],
                out=small[:, S_M2:S_M2 + 1], in_=aren32[:, 2064:2072], axis=AX.X)
            vop("tensor_scalar", reads=[B("rprob"), B("m2")], writes=[B("mask")],
                out=aren32[:, 2056:2064], in0=aren32[:, 2048:2056],
                scalar1=small[:, S_M2:S_M2 + 1], scalar2=None, op0=ALU.is_ge)
            vop("tensor_tensor", reads=[B("mask"), B("rprob")], writes=[B("localp")],
                out=lp, in0=aren32[:, 2048:2056], in1=aren32[:, 2056:2064],
                op=ALU.mult)
        GZ = Z1
        W1G, W1U = Z1 + 12288, Z1 + 14336
        W2Z = Z2 + 6144
        GBF = Z3 + 1024
        halves = [(0, 6), (6, 11)]
        for e in range(8):
            for (f0, f1) in halves:
                nfh = f1 - f0
                for fc in range(f0, f1, 2):
                    nft = min(2, f1 - fc)
                    ncol = nft * 128
                    for (zz, base, bn) in ((W1G, 0, "w1g"), (W1U, 1408, "w1u")):
                        dma("gpsimd",
                            AP(arena.tensor, zz,
                               [[AB, 128], [256, NF], [1, ncol]]),
                            AP(dr["L1_w1"].tensor, e * D * 2816 + base + fc * 128,
                               [[2816, 128], [2816 * 128, NF], [1, ncol]]),
                            reads=[], writes=[B(bn)])
                    for fo in range(nft):
                        ft = fc + fo
                        for tc in range(4):
                            for f in range(NF):
                                mm(ps_o[:, 0:512],
                                   A(W1G + f * 256 + fo * 128, 128),
                                   xT_f(f, tc * 512, (tc + 1) * 512),
                                   start=(f == 0), stop=(f == NF - 1),
                                   reads=[B("w1g"), B("xT")], writes=[B("ps_oz")])
                            for f in range(NF):
                                mm(ps_a[:, 0:512],
                                   A(W1U + f * 256 + fo * 128, 128),
                                   xT_f(f, tc * 512, (tc + 1) * 512),
                                   start=(f == 0), stop=(f == NF - 1),
                                   reads=[B("w1u"), B("xT")], writes=[B("ps_az")])
                            gsl = tc % 2
                            act(A(GBF + gsl * 512, 512), ps_o[:, 0:512], AF.Silu,
                                reads=[B("ps_oz")], writes=[B(f"gbf{gsl}")])
                            vop("tensor_tensor",
                                reads=[B(f"gbf{gsl}"), B("ps_az")], writes=[B("gz")],
                                out=A(GZ + (ft - f0) * 2048 + tc * 512, 512),
                                in0=A(GBF + gsl * 512, 512), in1=ps_a[:, 0:512],
                                op=ALU.mult)
                dma("gpsimd", AP(arena.tensor, W2Z, [[AB, 128], [D, nfh], [1, D]]),
                    AP(dr["L1_w2"].tensor, e * 1408 * D + f0 * 128 * D,
                       [[D, 128], [128 * D, nfh], [1, D]]),
                    reads=[], writes=[B("w2z")])
                for ti in range(NT):
                    for ch in range(2):
                        o = ps_s[:, ch * 512:(ch + 1) * 512]
                        for fo in range(nfh):
                            mm(o, A(GZ + fo * 2048 + ti * 128, 128),
                               A(W2Z + fo * D + ch * 512, 512),
                               start=(fo == 0), stop=(fo == nfh - 1),
                               reads=[B("gz"), B("w2z")], writes=[B("ps_sz")])
                    vop("scalar_tensor_tensor",
                        reads=[B("ps_sz"), B("localp"), B("xres")], writes=[B("xres")],
                        out=xres[:, ti * D:(ti + 1) * D], in0=ps_s[:, 0:1024],
                        scalar=localp[:, ti * 8 + e:ti * 8 + e + 1],
                        in1=xres[:, ti * D:(ti + 1) * D],
                        op0=ALU.mult, op1=ALU.add)

    def attn_l2():
        QTQ, AST2, WS2 = Z3, Z3 + 1024, Z3 + 2048
        for f in range(NF):
            dma("gpsimd", AP(arena.tensor, Z1 + f * 1024, [[AB, 128], [1, D]]),
                AP(dr["L2_wk_nat"].tensor, f * 128 * D, [[D, 128], [1, D]]),
                reads=[], writes=[B("wkres")])
        for ct in range(NF):
            dma("gpsimd", AP(arena.tensor, WS2, [[AB, 128], [128, NF], [1, 128]]),
                AP(dr["L2_wq_nat"].tensor, ct * 128, [[D, 128], [128 * D, NF], [1, 128]]),
                reads=[], writes=[B("wqslab")])
            for ch in range(2):
                o = ps_s[:, ch * 512:(ch + 1) * 512]
                for et in range(NF):
                    mm(o, A(WS2 + et * 128, 128),
                       arena[:, Z1 + et * 1024 + ch * 512:Z1 + et * 1024 + (ch + 1) * 512],
                       start=(et == 0), stop=(et == NF - 1),
                       reads=[B("wqslab"), B("wkres")], writes=[B("ps_sz")])
            act(A(Z2 + ct * 1024, 512), ps_s[:, 0:512], AF.Copy,
                reads=[B("ps_sz")], writes=[B("Mz")])
            act(A(Z2 + ct * 1024 + 512, 512), ps_s[:, 512:1024], AF.Copy,
                reads=[B("ps_sz")], writes=[B("Mz")])
        for dt_ in range(NF):
            for tc in range(4):
                o = ps_o[:, 0:512]
                for f in range(NF):
                    mm(o, A(Z2 + f * 1024 + dt_ * 128, 128),
                       xT_f(f, tc * 512, (tc + 1) * 512),
                       start=(f == 0), stop=(f == NF - 1),
                       reads=[B("Mz"), B("xT")], writes=[B("ps_oz")])
                act(arena[:, Z1 + dt_ * 2048 + tc * 512:Z1 + dt_ * 2048 + (tc + 1) * 512],
                    ps_o[:, 0:512], AF.Copy, reads=[B("ps_oz")], writes=[B("QMT")])
        for qt in range(4):
            dma("gpsimd", AP(arena.tensor, WS2, [[AB, 128], [256, NF], [1, 256]]),
                AP(dr["L2_wvT"].tensor, qt * 256, [[D, 128], [128 * D, NF], [1, 256]]),
                reads=[B("Mz")], writes=[B("wvslab")])
            for ti in range(NT):
                o = ps_o[:, 0:256]
                for f in range(NF):
                    mm(o, xT_f(f, ti * 128, ti * 128 + 128), A(WS2 + f * 256, 256),
                       start=(f == 0), stop=(f == NF - 1),
                       reads=[B("wvslab"), B("xT")], writes=[B("ps_oz")])
                act(A(Z2 + ti * 1024 + qt * 256, 256), ps_o[:, 0:256], AF.Copy,
                    reads=[B("ps_oz")], writes=[B("Vz")])
        for qi in range(NT):
            q0 = qi * 128
            kw = q0 + 128
            nck = cdiv(kw, 512)
            for ck in range(nck):
                c0, c1 = ck * 512, min(kw, (ck + 1) * 512)
                for dt_ in range(NF):
                    mm(ps_s[:, c0:c1], arena[:, Z1 + dt_ * 2048 + q0:Z1 + dt_ * 2048 + q0 + 128],
                       xT_f(dt_, c0, c1),
                       start=(dt_ == 0), stop=(dt_ == NF - 1),
                       reads=[B("QMT"), B("xT")], writes=[B("ps_sz")])
            vop("reduce_max", reads=[B("ps_sz")], writes=[B("negmax")],
                out=small[:, S_NEGMAX:S_NEGMAX + 1], in_=ps_s[:, 0:kw],
                axis=AX.X, negate=True)
            vop("tensor_scalar", reads=[B("negmax")], writes=[B("ebias")],
                out=small[:, S_EB:S_EB + 1], in0=small[:, S_NEGMAX:S_NEGMAX + 1],
                scalar1=0.03125, scalar2=None, op0=ALU.mult)
            for ck in range(nck):
                c0, c1 = ck * 512, min(kw, (ck + 1) * 512)
                act(prb_t[:, c0:c1], ps_s[:, c0:c1], AF.Exp,
                    reads=[B("ps_sz"), B("ebias")], writes=[B("PRB")],
                    bias=small[:, S_EB:S_EB + 1], scale=0.03125)
            vop("tensor_tensor", reads=[B("PRB"), B("ctrilb")], writes=[B("PRB")],
                out=prb_t[:, q0:q0 + 128], in0=prb_t[:, q0:q0 + 128], in1=trilb[:, :],
                op=ALU.mult)
            vop("reduce_sum", reads=[B("PRB")], writes=[B("sum2")],
                out=small[:, S_SUM:S_SUM + 1], in_=prb_t[:, 0:kw], axis=AX.X)
            vop("reciprocal", reads=[B("sum2")], writes=[B("recip")],
                out=small[:, S_RECIP:S_RECIP + 1], in_=small[:, S_SUM:S_SUM + 1])
            vop("tensor_scalar", reads=[B("PRB"), B("recip")], writes=[B("PRB")],
                out=prb_t[:, 0:kw], in0=prb_t[:, 0:kw],
                scalar1=small[:, S_RECIP:S_RECIP + 1], scalar2=None, op0=ALU.mult)
            nkt = kw // 128
            xpose(AP(prt_t.tensor, 0, [[2048, 128], [128, 16], [1, 128]]),
                  AP(prb_t.tensor, 0, [[2048, 128], [128, 16], [1, 128]]),
                  reads=[B("PRB")], writes=[B("PRTT")])
            for dc in range(NF):
                for kt in range(nkt):
                    mm(ps_s[:, dc * 128:(dc + 1) * 128],
                       AP(arena.tensor, Z2 + kt * 1024 + dc * 128, [[AB, 128], [1, 128]]),
                       prt_t[:, kt * 128:(kt + 1) * 128],
                       start=(kt == 0), stop=(kt == nkt - 1),
                       reads=[B("Vz"), B("PRTT")], writes=[B("ps_sz")])
            act(A(AST2, 512), ps_s[:, 0:512], AF.Copy,
                reads=[B("ps_sz")], writes=[B("at2")])
            act(A(AST2 + 512, 512), ps_s[:, 512:1024], AF.Copy,
                reads=[B("ps_sz")], writes=[B("at2")])
            dma("sync", AP(att_d.tensor, qi * H * HD * 128, [[1024, 128], [1, 1024]]),
                A(AST2, 1024), reads=[B("at2")], writes=[B("attd")])
        for f in range(NF):
            dma("gpsimd", AP(arena.tensor, Z2 + f * 1024, [[AB, 128], [1, D]]),
                AP(dr["L2_woT"].tensor, f * 128 * D, [[D, 128], [1, D]]),
                reads=[B("Vz")], writes=[B("Vz")])
        for qi in range(NT):
            dma("gpsimd", A(QTQ, 1024),
                AP(att_d.tensor, qi * H * HD * 128, [[1024, 128], [1, 1024]]),
                reads=[B("attd")], writes=[B("atw2")])
            for ch in range(2):
                o = ps_s[:, ch * 512:(ch + 1) * 512]
                for f in range(NF):
                    mm(o, A(QTQ + f * 128, 128),
                       arena[:, Z2 + f * 1024 + ch * 512:Z2 + f * 1024 + (ch + 1) * 512],
                       start=(f == 0), stop=(f == NF - 1),
                       reads=[B("atw2"), B("Vz")], writes=[B("ps_sz")])
            vop("tensor_tensor", reads=[B("ps_sz"), B("xres")], writes=[B("xres")],
                out=xres[:, qi * D:(qi + 1) * D], in0=ps_s[:, 0:1024],
                in1=xres[:, qi * D:(qi + 1) * D], op=ALU.add)

    def ffn_l2():
        for (wi, wn, rows, cols) in ((0, "L2_wgT", D, 4096), (1, "L2_wuT", D, 4096),
                                     (2, "L2_wdT", 4096, D)):
            nrt = rows // 128
            ncc = 4096 // cols if False else None
            colw = 4096 // NF if cols == 4096 else D
            nchunk = (rows * cols) // (128 * NF * colw) if False else None
            chunks = []
            if cols == 4096:
                for cc in range(16):
                    chunks.append((0, nrt, cc * 256, 256))
            else:
                for rg in range(nrt // 2):
                    chunks.append((rg * 2, 2, 0, 1024))
            vop("memset", reads=[], writes=[B("wacc")],
                ap=small[:, S_ACC:S_ACC + 1], constant=0.0)
            for (r0, nr, c0, cw) in chunks:
                dma("gpsimd", AP(aren32.tensor, 0, [[4096, 128], [cw, nr], [1, cw]]),
                    AP(dr[wn].tensor, r0 * 128 * cols + c0,
                       [[cols, 128], [128 * cols, nr], [1, cw]]),
                    reads=[B("wf32")], writes=[B("wf32")])
                vop("reduce_sum", reads=[B("wf32")], writes=[B("wrs")],
                    out=small[:, S_TOT:S_TOT + 1], in_=aren32[:, 0:nr * cw],
                    axis=AX.X, apply_absolute_value=True)
                vop("tensor_tensor", reads=[B("wrs"), B("wacc")], writes=[B("wacc")],
                    out=small[:, S_ACC:S_ACC + 1], in0=small[:, S_TOT:S_TOT + 1],
                    in1=small[:, S_ACC:S_ACC + 1], op=ALU.add)
            mm(ps_o[0:1, 0:1], small[:, S_ONE:S_ONE + 1], small[:, S_ACC:S_ACC + 1],
               True, True, reads=[B("wacc"), B("onefc")], writes=[B("ps_oz")])
            vop("tensor_copy", reads=[B("ps_oz")], writes=[B("wtot")],
                out=small[0:1, S_TOT:S_TOT + 1], in_=ps_o[0:1, 0:1])
            vop("tensor_copy", reads=[B("wtot")], writes=[B("onerow")],
                out=localp[0:1, 0:128],
                in_=AP(small.tensor, S_TOT, [[64, 1], [0, 128]]))
            vop("tensor_scalar", reads=[B("onerow")], writes=[B("onerow")],
                out=localp[0:1, 0:128], in0=localp[0:1, 0:128],
                scalar1=0.0, scalar2=1.0, op0=ALU.mult, op1=ALU.add)
            mm(ps_o[:, 0:1], localp[0:1, 0:128], small[0:1, S_TOT:S_TOT + 1],
               True, True, reads=[B("onerow"), B("wtot")], writes=[B("ps_oz")])
            vop("tensor_scalar", reads=[B("ps_oz")], writes=[B("wmean")],
                out=small[:, S_MW + wi:S_MW + wi + 1],
                in0=ps_o[:, 0:1],
                scalar1=1.0 / (rows * cols), scalar2=1e-5, op0=ALU.mult, op1=ALU.max)
            vop("reciprocal", reads=[B("wmean")], writes=[B("wsw")],
                out=small[:, S_MW + 3:S_MW + 4], in_=small[:, S_MW + wi:S_MW + wi + 1])
            for (r0, nr, c0, cw) in chunks:
                dma("gpsimd", AP(aren32.tensor, 0, [[4096, 128], [cw, nr], [1, cw]]),
                    AP(dr[wn].tensor, r0 * 128 * cols + c0,
                       [[cols, 128], [128 * cols, nr], [1, cw]]),
                    reads=[B("wf32")], writes=[B("wf32")])
                for sub in range(nr * cw // 1024):
                    sc_ = aren32[:, sub * 1024:(sub + 1) * 1024]
                    yf = aren32[:, 2048:3072]
                    rr = aren32[:, 3072:4096]
                    act(areni[:, 0:1024], sc_, AF.Copy,
                        reads=[B("wf32"), B("wsw")], writes=[B("wint")],
                        scale=small[:, S_MW + 3:S_MW + 4])
                    act(yf, areni[:, 0:1024], AF.Copy,
                        reads=[B("wint")], writes=[B("yf")])
                    vop("scalar_tensor_tensor", reads=[B("wf32"), B("wsw"), B("yf")],
                        writes=[B("rr")], out=rr, in0=sc_,
                        scalar=small[:, S_MW + 3:S_MW + 4], in1=yf,
                        op0=ALU.mult, op1=ALU.subtract)
                    vop("scalar_tensor_tensor", reads=[B("rr"), B("yf")], writes=[B("yf")],
                        out=yf, in0=rr, scalar=0.5, in1=yf,
                        op0=ALU.is_gt, op1=ALU.add)
                    vop("tensor_scalar", reads=[B("rr")], writes=[B("rr")],
                        out=rr, in0=rr, scalar1=-0.5, scalar2=None, op0=ALU.is_lt)
                    vop("tensor_tensor", reads=[B("yf"), B("rr")], writes=[B("yf")],
                        out=yf, in0=yf, in1=rr, op=ALU.subtract)
                    vop("tensor_scalar", reads=[B("yf")], writes=[B("z10")],
                        out=A(Z1 + sub * 1024, 1024), in0=yf,
                        scalar1=1.0, scalar2=-1.0, op0=ALU.min, op1=ALU.max)
                nsub = nr * cw // 1024
                dma("sync", AP(wq_d.tensor, wi * 4096 * D + r0 * 128 * cols + c0,
                               [[cols, 128], [128 * cols, nr], [1, cw]]),
                    AP(arena.tensor, Z1, [[AB, 128], [cw, nr], [1, cw]]),
                    reads=[B("z10")], writes=[B("wqd")])
        for ti in range(NT):
            xs = xres[:, ti * D:(ti + 1) * D]
            vop("reduce_max", reads=[B("xres")], writes=[B("mxa")],
                out=small[:, S_MXA:S_MXA + 1], in_=xs, axis=AX.X,
                apply_absolute_value=True)
            vop("tensor_scalar", reads=[B("mxa")], writes=[B("mxa")],
                out=small[:, S_MXA:S_MXA + 1], in0=small[:, S_MXA:S_MXA + 1],
                scalar1=1e-5, scalar2=None, op0=ALU.max)
            vop("reciprocal", reads=[B("mxa")], writes=[B("sxr")],
                out=small[:, S_SX:S_SX + 1], in_=small[:, S_MXA:S_MXA + 1])
            vop("tensor_scalar", reads=[B("sxr")], writes=[B("sxr")],
                out=small[:, S_SX:S_SX + 1], in0=small[:, S_SX:S_SX + 1],
                scalar1=127.0, scalar2=None, op0=ALU.mult)
            vop("tensor_scalar", reads=[B("mxa"), B("wmean")], writes=[B("oscg")],
                out=sxcol[:, ti:ti + 1], in0=small[:, S_MXA:S_MXA + 1],
                scalar1=small[:, S_MW:S_MW + 1], scalar2=1.0 / 127.0,
                op0=ALU.mult, op1=ALU.mult)
            vop("tensor_scalar", reads=[B("mxa"), B("wmean")], writes=[B("oscu")],
                out=sxcol[:, 16 + ti:16 + ti + 1], in0=small[:, S_MXA:S_MXA + 1],
                scalar1=small[:, S_MW + 1:S_MW + 2], scalar2=1.0 / 127.0,
                op0=ALU.mult, op1=ALU.mult)
            act(areni[:, 0:1024], xs, AF.Copy,
                reads=[B("xres"), B("sxr")], writes=[B("xint")],
                scale=small[:, S_SX:S_SX + 1])
            act(aren32[:, 0:1024], areni[:, 0:1024], AF.Copy,
                reads=[B("xint")], writes=[B("yf")])
            vop("scalar_tensor_tensor", reads=[B("xres"), B("sxr"), B("yf")],
                writes=[B("rr")], out=aren32[:, 1024:2048], in0=xs,
                scalar=small[:, S_SX:S_SX + 1], in1=aren32[:, 0:1024],
                op0=ALU.mult, op1=ALU.subtract)
            vop("scalar_tensor_tensor", reads=[B("rr"), B("yf")], writes=[B("yf")],
                out=aren32[:, 0:1024], in0=aren32[:, 1024:2048], scalar=0.5,
                in1=aren32[:, 0:1024], op0=ALU.is_gt, op1=ALU.add)
            vop("tensor_scalar", reads=[B("rr")], writes=[B("rr")],
                out=aren32[:, 1024:2048], in0=aren32[:, 1024:2048],
                scalar1=-0.5, scalar2=None, op0=ALU.is_lt)
            sb = ti % 2
            vop("tensor_tensor", reads=[B("yf"), B("rr")], writes=[B(f"xbf{sb}")],
                out=xbfs[sb][:, :], in0=aren32[:, 0:1024],
                in1=aren32[:, 1024:2048], op=ALU.subtract)
            sv = xbfs[sb].rearrange("p (j q) -> p j q", q=128)
            dv = AP(ts1024.tensor, 0, [[D, 128], [128, 8], [1, 128]])
            xpose(dv, sv, reads=[B(f"xbf{sb}")], writes=[B("ts1024")])
            vop("tensor_copy", reads=[B("ts1024")], writes=[B("xT")],
                out=AP(arena.tensor, XT + ti * 128, [[AB, 128], [2048, 8], [1, 128]]),
                in_=AP(ts1024.tensor, 0, [[D, 128], [128, 8], [1, 128]]))
        G2 = 2048
        for g in range(8):
            tis = [2 * g, 2 * g + 1]
            for oc in range(8):
                for (zz, wi, bn) in ((Z1, 0, "z1"), (Z1 + 8192, 1, "z2h")):
                    dma("sync", AP(arena.tensor, zz + (oc % 2) * 4096,
                                   [[AB, 128], [512, NF], [1, 512]]),
                        AP(wq_d.tensor, wi * 4096 * D + oc * 512,
                           [[4096, 128], [128 * 4096, NF], [1, 512]]),
                        reads=[], writes=[B(f"{bn}{oc % 2}")])
                for si, ti in enumerate(tis):
                    for f in range(NF):
                        mm(ps_o[:, 0:512],
                           xT_f(f, ti * 128, ti * 128 + 128),
                           A(Z1 + (oc % 2) * 4096 + f * 512, 512),
                           start=(f == 0), stop=(f == NF - 1),
                           reads=[B(f"z1{oc % 2}"), B("xT")], writes=[B("ps_oz")])
                    for f in range(NF):
                        mm(ps_a[:, 0:512],
                           xT_f(f, ti * 128, ti * 128 + 128),
                           A(Z1 + 8192 + (oc % 2) * 4096 + f * 512, 512),
                           start=(f == 0), stop=(f == NF - 1),
                           reads=[B(f"z2h{oc % 2}"), B("xT")], writes=[B("ps_az")])
                    act(A(Z3 + si * 512, 512), ps_o[:, 0:512], AF.Silu,
                        reads=[B("ps_oz"), B("oscg")], writes=[B(f"sil{si}")],
                        scale=sxcol[:, ti:ti + 1])
                    vop("scalar_tensor_tensor",
                        reads=[B("ps_az"), B(f"sil{si}"), B("oscu")], writes=[B("g2b")],
                        out=A(Z2 + si * 4096 + oc * 512, 512), in0=ps_a[:, 0:512],
                        scalar=sxcol[:, 16 + ti:16 + ti + 1],
                        in1=A(Z3 + si * 512, 512), op0=ALU.mult, op1=ALU.mult)
            for si, ti in enumerate(tis):
                vop("reduce_max", reads=[B("g2b")], writes=[B("mxa")],
                    out=small[:, S_MXA:S_MXA + 1], in_=A(Z2 + si * 4096, 4096),
                    axis=AX.X, apply_absolute_value=True)
                vop("tensor_scalar", reads=[B("mxa")], writes=[B("mxa")],
                    out=small[:, S_MXA:S_MXA + 1], in0=small[:, S_MXA:S_MXA + 1],
                    scalar1=1e-5, scalar2=None, op0=ALU.max)
                vop("reciprocal", reads=[B("mxa")], writes=[B("sxr")],
                    out=small[:, S_SX:S_SX + 1], in_=small[:, S_MXA:S_MXA + 1])
                vop("tensor_scalar", reads=[B("sxr")], writes=[B("sxr")],
                    out=small[:, S_SX:S_SX + 1], in0=small[:, S_SX:S_SX + 1],
                    scalar1=127.0, scalar2=None, op0=ALU.mult)
                vop("tensor_scalar", reads=[B("mxa"), B("wmean")], writes=[B("oscd")],
                    out=sxcol[:, 32 + ti:32 + ti + 1], in0=small[:, S_MXA:S_MXA + 1],
                    scalar1=small[:, S_MW + 2:S_MW + 3], scalar2=1.0 / 127.0,
                    op0=ALU.mult, op1=ALU.mult)
                for piece in range(2):
                    for sub in range(2):
                        g2src = A(Z2 + si * 4096 + piece * 2048 + sub * 1024, 1024)
                        act(areni[:, 0:1024], g2src, AF.Copy,
                            reads=[B("g2b"), B("sxr")], writes=[B("xint")],
                            scale=small[:, S_SX:S_SX + 1])
                        act(aren32[:, 0:1024], areni[:, 0:1024], AF.Copy,
                            reads=[B("xint")], writes=[B("yf")])
                        vop("scalar_tensor_tensor",
                            reads=[B("g2b"), B("sxr"), B("yf")], writes=[B("rr")],
                            out=aren32[:, 1024:2048], in0=g2src,
                            scalar=small[:, S_SX:S_SX + 1], in1=aren32[:, 0:1024],
                            op0=ALU.mult, op1=ALU.subtract)
                        vop("scalar_tensor_tensor", reads=[B("rr"), B("yf")],
                            writes=[B("yf")],
                            out=aren32[:, 0:1024], in0=aren32[:, 1024:2048],
                            scalar=0.5, in1=aren32[:, 0:1024],
                            op0=ALU.is_gt, op1=ALU.add)
                        vop("tensor_scalar", reads=[B("rr")], writes=[B("rr")],
                            out=aren32[:, 1024:2048], in0=aren32[:, 1024:2048],
                            scalar1=-0.5, scalar2=None, op0=ALU.is_lt)
                        vop("tensor_tensor", reads=[B("yf"), B("rr")], writes=[B("PRB")],
                            out=prb_t[:, sub * 1024:(sub + 1) * 1024],
                            in0=aren32[:, 0:1024], in1=aren32[:, 1024:2048],
                            op=ALU.subtract)
                    xpose(AP(prt_t.tensor, 0, [[2048, 128], [128, 16], [1, 128]]),
                          AP(prb_t.tensor, 0, [[2048, 128], [128, 16], [1, 128]]),
                          reads=[B("PRB")], writes=[B("PRTT")])
                    for wc in range(4):
                        wci = piece * 4 + wc
                        dma("sync", AP(arena.tensor, Z1 + (wc % 2) * 4096,
                                       [[AB, 128], [D, 4], [1, D]]),
                            AP(wq_d.tensor, 2 * 4096 * D + wci * 4 * 128 * D,
                               [[D, 128], [128 * D, 4], [1, D]]),
                            reads=[], writes=[B(f"z1{wc % 2}")])
                        for ch in range(2):
                            o = ps_s[:, si * 1024 + ch * 512: si * 1024 + (ch + 1) * 512]
                            for cc in range(4):
                                mm(o, prt_t[:, (wc * 4 + cc) * 128:(wc * 4 + cc) * 128 + 128],
                                   A(Z1 + (wc % 2) * 4096 + cc * D + ch * 512, 512),
                                   start=(piece == 0 and wc == 0 and cc == 0),
                                   stop=(piece == 1 and wc == 3 and cc == 3),
                                   reads=[B("PRTT"), B(f"z1{wc % 2}")],
                                   writes=[B("ps_sz")])
            for si, ti in enumerate(tis):
                vop("scalar_tensor_tensor",
                    reads=[B("ps_sz"), B("oscd"), B("xres")], writes=[B("xres")],
                    out=xres[:, ti * D:(ti + 1) * D],
                    in0=ps_s[:, si * 1024:(si + 1) * 1024],
                    scalar=sxcol[:, 32 + ti:32 + ti + 1],
                    in1=xres[:, ti * D:(ti + 1) * D], op0=ALU.mult, op1=ALU.add)

    def attn_l3():
        W1TZ, W2TZ, GEL, WS3 = Z3, Z3 + 512, Z3 + 1536, Z2
        for dt_ in range(NF):
            dma("gpsimd", AP(arena.tensor, WS3, [[AB, 128], [128, NF], [1, 128]]),
                AP(dr["L3_wKT"].tensor, dt_ * 128, [[D, 128], [128 * D, NF], [1, 128]]),
                reads=[], writes=[B("wk3")])
            for tc in range(4):
                o = ps_o[:, 0:512]
                for f in range(NF):
                    mm(o, A(WS3 + f * 128, 128),
                       xT_f(f, tc * 512, (tc + 1) * 512),
                       start=(f == 0), stop=(f == NF - 1),
                       reads=[B("wk3"), B("xT")], writes=[B("ps_oz")])
                act(arena[:, Z1 + dt_ * 2048 + tc * 512:Z1 + dt_ * 2048 + (tc + 1) * 512],
                    ps_o[:, 0:512], AF.Copy, reads=[B("ps_oz")], writes=[B("KT3")])
        dma("gpsimd", AP(arena.tensor, W1TZ, [[AB, 128], [HD, NF], [1, HD]]),
            AP(dr["L3_w1T"].tensor, 0, [[HD, 128], [128 * HD, NF], [1, HD]]),
            reads=[], writes=[B("w1t")])
        for tc in range(4):
            o = ps_o[0:64, 0:512]
            for f in range(NF):
                mm(o, A(W1TZ + f * HD, HD),
                   arena[:, Z1 + f * 2048 + tc * 512:Z1 + f * 2048 + (tc + 1) * 512],
                   start=(f == 0), stop=(f == NF - 1),
                   reads=[B("w1t"), B("KT3")], writes=[B("ps_oz")])
            act(arena[0:64, GEL + tc * 512:GEL + (tc + 1) * 512],
                ps_o[0:64, 0:512], AF.Gelu, reads=[B("ps_oz")], writes=[B("gel")])
        dma("gpsimd", AP(arena.tensor, W2TZ, [[AB, 64], [1, D]]),
            AP(dr["L3_w2T"].tensor, 0, [[D, 64], [1, D]]),
            reads=[], writes=[B("w2t")])
        for ti in range(NT):
            for ch in range(2):
                o = ps_s[:, ch * 512:(ch + 1) * 512]
                mm(o, arena[0:64, GEL + ti * 128:GEL + ti * 128 + 128],
                   arena[0:64, W2TZ + ch * 512:W2TZ + (ch + 1) * 512],
                   start=True, stop=True,
                   reads=[B("gel"), B("w2t")], writes=[B("ps_sz")])
            vop("tensor_tensor", reads=[B("ps_sz"), B("xres")], writes=[B("xres")],
                out=xres[:, ti * D:(ti + 1) * D], in0=ps_s[:, 0:1024],
                in1=xres[:, ti * D:(ti + 1) * D], op=ALU.add)

    def ffn_l3():
        WS3 = Z3
        dma("gpsimd", dml[:, S_MU:S_MU + NF],
            AP(dr["L3_mur"].tensor, 0, [[NF, 128], [1, NF]]),
            reads=[], writes=[B("mu")])
        dma("gpsimd", dml[:, S_MUK:S_MUK + NF],
            AP(dr["L3_muk"].tensor, 0, [[NF, 128], [1, NF]]),
            reads=[], writes=[B("mu")])
        vop("tensor_scalar", reads=[B("mu")], writes=[B("mu1")],
            out=dml[:, S_MU1:S_MU1 + NF], in0=dml[:, S_MU:S_MU + NF],
            scalar1=-1.0, scalar2=1.0, op0=ALU.mult, op1=ALU.add)
        vop("tensor_scalar", reads=[B("mu")], writes=[B("mu1")],
            out=dml[:, S_MUK1:S_MUK1 + NF], in0=dml[:, S_MUK:S_MUK + NF],
            scalar1=-1.0, scalar2=1.0, op0=ALU.mult, op1=ALU.add)
        for (zone, mc, m1c, bn) in ((Z1, S_MU, S_MU1, "xrT"), (Z2, S_MUK, S_MUK1, "xkT")):
            for f in range(NF):
                vop("tensor_scalar", reads=[B("xT"), B("mu"), B("mu1")], writes=[B(bn)],
                    out=A(zone + f * 2048, 1), in0=xT_f(f, 0, 1),
                    scalar1=dml[:, mc + f:mc + f + 1], scalar2=None, op0=ALU.mult)
                vop("tensor_scalar", reads=[B("xT"), B("mu")], writes=[B(bn)],
                    out=A(zone + f * 2048 + 1, 2047), in0=xT_f(f, 1, 2048),
                    scalar1=dml[:, mc + f:mc + f + 1], scalar2=None, op0=ALU.mult)
                vop("scalar_tensor_tensor", reads=[B("xT"), B("mu1"), B(bn)],
                    writes=[B(bn)],
                    out=A(zone + f * 2048 + 1, 2047), in0=xT_f(f, 0, 2047),
                    scalar=dml[:, m1c + f:m1c + f + 1],
                    in1=A(zone + f * 2048 + 1, 2047), op0=ALU.mult, op1=ALU.add)
        for dt_ in range(NF):
            sl = dt_ % 2
            dma("gpsimd", AP(arena.tensor, WS3 + sl * 1024, [[AB, 128], [128, NF], [1, 128]]),
                AP(dr["L3_wkT3"].tensor, dt_ * 128, [[D, 128], [128 * D, NF], [1, 128]]),
                reads=[], writes=[B(f"wl3{sl}")])
            for tc in range(4):
                o = ps_o[:, 0:512]
                for f in range(NF):
                    mm(o, A(WS3 + sl * 1024 + f * 128, 128),
                       A(Z2 + f * 2048 + tc * 512, 512),
                       start=(f == 0), stop=(f == NF - 1),
                       reads=[B(f"wl3{sl}"), B("xkT")], writes=[B("ps_oz")])
                act(aren32[:, 0:512], ps_o[:, 0:512], AF.Relu,
                    reads=[B("ps_oz")], writes=[B("SC")])
                vop("tensor_tensor", reads=[B("SC")], writes=[B("xT")],
                    out=arena[:, XT + dt_ * 2048 + tc * 512:XT + dt_ * 2048 + (tc + 1) * 512],
                    in0=aren32[:, 0:512], in1=aren32[:, 0:512], op=ALU.mult)
        if _os.environ.get("KB_L3STOP", "") == "1":
            return
        for hf in range(2):
            dma("gpsimd",
                AP(arena.tensor, Z2 + hf * 4096, [[AB, 128], [512, NF], [1, 512]]),
                AP(dr["L3_wrT"].tensor, hf * 512, [[D, 128], [128 * D, NF], [1, 512]]),
                reads=[], writes=[B("xkT"), B(f"wr3{hf}")])
            dma("gpsimd",
                AP(arena.tensor, Z2 + 8192 + hf * 4096, [[AB, 128], [512, NF], [1, 512]]),
                AP(dr["L3_wvT3"].tensor, hf * 512, [[D, 128], [128 * D, NF], [1, 512]]),
                reads=[], writes=[B("xkT"), B(f"wv3{hf}")])
        if _os.environ.get("KB_L3STOP", "") == "2":
            return
        for ti in range(NT):
            for hf in range(2):
                o = ps_o[:, 0:512]
                for f in range(NF):
                    mm(o, A(Z1 + f * 2048 + ti * 128, 128),
                       A(Z2 + hf * 4096 + f * 512, 512),
                       start=(f == 0), stop=(f == NF - 1),
                       reads=[B(f"wr3{hf}"), B("xrT")], writes=[B("ps_oz")])
                act(aren32[:, 1024:1536], ps_o[:, 0:512], AF.Sigmoid,
                    reads=[B("ps_oz")], writes=[B("rstg")])
                o2 = ps_a[:, 0:512]
                for f in range(NF):
                    mm(o2, arena[:, XT + f * 2048 + ti * 128:XT + f * 2048 + ti * 128 + 128],
                       AP(arena.tensor, Z2 + 8192 + hf * 4096 + f * 512, [[AB, 128], [1, 512]]),
                       start=(f == 0), stop=(f == NF - 1),
                       reads=[B(f"wv3{hf}"), B("xT")], writes=[B("ps_az")])
                vop("tensor_tensor", reads=[B("ps_az"), B("rstg")], writes=[B("SC")],
                    out=aren32[:, 0:512], in0=ps_a[:, 0:512],
                    in1=aren32[:, 1024:1536], op=ALU.mult)
                vop("tensor_tensor", reads=[B("SC"), B("xres")], writes=[B("xres")],
                    out=xres[:, ti * D + hf * 512:ti * D + (hf + 1) * 512],
                    in0=aren32[:, 0:512],
                    in1=xres[:, ti * D + hf * 512:ti * D + (hf + 1) * 512],
                    op=ALU.add)

    subs = [lambda: attn_layer(0), ffn_l0, lambda: attn_layer(1), ffn_l1,
            attn_l2, ffn_l2, attn_l3, ffn_l3]
    for i in range(n_sub):
        subs[i]()
        ln_all(last=(i == n_sub - 1))
        barrier()

    import os
    nops = int(os.environ.get("KB_NOPS", "0"))
    if nops > 0:
        p.ops = p.ops[:nops]
        for b in bufs.values():
            b.rs = [r for r in b.rs if r < nops]
            if b.w is not None and b.w >= nops:
                b.w = None

    if KB_DBG == "":
        st = p.add("sync",
                   lambda e: e.dma_start(
                       out=AP(out.tensor, 0, [[D, 128], [D * 128, NT], [1, D]]),
                       in_=AP(xres.tensor, 0, [[NT * D, 128], [D, NT], [1, D]])),
                   reads=[B("xres")], writes=[], is_dma=True)
        p.ops[st]["inc"] = True
    else:
        st = p.add("sync",
                   lambda e: e.dma_start(
                       out=AP(out.tensor, 1024 * D, [[D, 64], [1, D]]),
                       in_=xres[0:64, 0:1024]),
                   reads=[B("xres")], writes=[], is_dma=True)
        p.ops[st]["inc"] = True

    p.emit_plan()
    sems = {}
    for e in Prog.ENGINES:
        sems[("c", e)] = nc.alloc_semaphore(name=f"sc_{e}")
        for sl in range(Prog.NSLOT):
            sems[("d", e, sl)] = nc.alloc_semaphore(name=f"sd_{e}{sl}")

    with nc.Block() as block:
        @block.tensor
        def _(eng):
            p.emit_engine("tensor", eng, sems)

        @block.vector
        def _(eng):
            p.emit_engine("vector", eng, sems)

        @block.scalar
        def _(eng):
            p.emit_engine("scalar", eng, sems)

        @block.gpsimd
        def _(eng):
            p.emit_engine("gpsimd", eng, sems)

        @block.sync
        def _(eng):
            p.emit_engine("sync", eng, sems)
            for sl in range(Prog.NSLOT):
                key = ("d", "sync", sl)
                if key in p.counters:
                    eng.wait_ge(sems[key], p.counters[key])
    return nc


def host_inputs(inputs, n_sub=N_SUB):
    C = np.ascontiguousarray
    f32 = np.float32

    def hm(w):
        return C(w.T.reshape(D, H, HD).transpose(1, 0, 2))

    def wor(w):
        return C(w.T.reshape(H, HD, D).transpose(1, 0, 2).reshape(HD, H * D))

    g = lambda n: np.asarray(inputs[n], dtype=f32)
    m = {}
    m["L0_wqT_hm"] = hm(g("L0_attn_Wq"))
    m["L0_wkT_hm"] = hm(g("L0_attn_Wk"))
    m["L0_wvT_hm"] = hm(g("L0_attn_Wv"))
    m["L0_woT_r"] = wor(g("L0_attn_Wo"))
    pek = g("L0_attn_pek")
    pev = g("L0_attn_pev")
    m["L0_pekT"] = C(pek[0:1025].T)
    m["L0_pevr"] = C(pev[1:1025])
    m["L0_pev0c"] = C(pev[0].reshape(HD, 1))
    m["L0_pev0r"] = C(pev[0].reshape(1, HD))
    m["L0_w"] = C(g("L0_ffn_weight"))
    m["L0_a1"] = C(g("L0_ffn_aux_weight1"))
    m["L0_a2"] = C(g("L0_ffn_aux_weight2"))
    m["L0_bias"] = C(g("L0_ffn_bias").reshape(1, D))
    m["L1_wqT_hm"] = hm(g("L1_attn_Wq"))
    m["L1_wkT_hm"] = hm(g("L1_attn_Wk"))
    m["L1_wvT_hm"] = hm(g("L1_attn_Wv"))
    m["L1_woT_r"] = wor(g("L1_attn_Wo"))
    m["L1_bq_r"] = C(g("L1_attn_bq").reshape(H, HD).T)
    m["L1_bk_r"] = C(g("L1_attn_bk").reshape(H, HD).T)
    m["L1_bv_r"] = C(g("L1_attn_bv").reshape(H, HD).T)
    m["L1_bo"] = C(g("L1_attn_bo").reshape(1, D))
    m["L1_wqsT"] = C(g("L1_attn_Wqs").T)
    m["L1_wksT"] = C(g("L1_attn_Wks").T)
    m["L1_wvsT"] = C(g("L1_attn_Wvs").T)
    m["L1_bqs_c"] = C(g("L1_attn_bqs").reshape(3, 1))
    m["L1_bks_r"] = C(g("L1_attn_bks").reshape(1, 3))
    m["L1_bvs_r"] = C(g("L1_attn_bvs").reshape(1, 3))
    m["L1_temp"] = C(g("L1_attn_temp").reshape(1, 1))
    m["L1_rw"] = C(g("L1_ffn_router_weight"))
    m["L1_w1"] = C(g("L1_ffn_w1"))
    m["L1_w2"] = C(g("L1_ffn_w2"))
    m["L2_wq_nat"] = C(g("L2_attn_Wq"))
    m["L2_wk_nat"] = C(g("L2_attn_Wk"))
    m["L2_wvT"] = C(g("L2_attn_Wv").T)
    m["L2_woT"] = C(g("L2_attn_Wo").T)
    m["L2_wgT"] = C(g("L2_ffn_w_gate").T)
    m["L2_wuT"] = C(g("L2_ffn_w_up").T)
    m["L2_wdT"] = C(g("L2_ffn_w_down").T)
    m["L3_wKT"] = C(g("L3_attn_W_K").T)
    m["L3_w1T"] = C(g("L3_attn_W1").T)
    m["L3_w2T"] = C(g("L3_attn_W2").T)
    m["L3_wrT"] = C(g("L3_ffn_W_r").T)
    m["L3_wkT3"] = C(g("L3_ffn_W_k").T)
    m["L3_wvT3"] = C(g("L3_ffn_W_v").T)
    m["L3_mur"] = C(g("L3_ffn_mu_r").reshape(NF, 128).T)
    m["L3_muk"] = C(g("L3_ffn_mu_k").reshape(NF, 128).T)
    m["c_ones"] = np.ones((128, 128), dtype=BF16NP)
    m["c_trilb"] = np.tril(np.ones((128, 128))).astype(BF16NP)
    m["c_tril"] = np.tril(np.ones((128, 128))).astype(f32)
    m["c_trilu"] = np.tril(np.ones((128, 128))).astype(np.uint8)
    m["c_ninf"] = ((1.0 - np.tril(np.ones((128, 128)))) * -1e5).astype(f32)
    return m


def kernel(**inputs):
    x = np.ascontiguousarray(np.asarray(inputs["L0_attn_x"], dtype=np.float32))
    ncores = x.shape[0]
    shared = host_inputs(inputs)
    nc = build_nc(N_SUB)
    in_maps = []
    for c in range(ncores):
        mcore = dict(shared)
        mcore["x"] = np.ascontiguousarray(x[c])
        in_maps.append(mcore)
    res = run_bass_kernel_spmd(nc, in_maps, core_ids=list(range(ncores)))
    outs = [res.results[c]["out"] for c in range(ncores)]
    return np.stack(outs, axis=0).astype(np.float32)


if __name__ == "__main__":
    pass
